# version 03 (final)
# baseline (speedup 1.0000x reference)
import math
from contextlib import ExitStack

import numpy as np
import ml_dtypes

import concourse.bass as bass
import concourse.mybir as mybir
from concourse.bass_utils import run_bass_kernel_spmd
from concourse.bass_core_rust import IndirectOffsetOnAxis

F32 = mybir.dt.float32
BF16 = mybir.dt.bfloat16
I16 = mybir.dt.int16
I32 = mybir.dt.int32
U32 = mybir.dt.uint32
AF = mybir.ActivationFunctionType
ALU = mybir.AluOpType
AXX = mybir.AxisListType.X

H = 16
DH = 64
N = 4096
C = 1024
E = 8
CAP = 3072
HID = 4096
NTILE = N // 128
KC = N // 128
QCP = 512
QH = QCP // 2
NQCP = N // QCP
EPS = 1e-5
ELO = float(np.float32(math.exp(-8.0)))
EHI = float(np.float32(math.exp(127.0 / 16.0)))
BF = ml_dtypes.bfloat16

ENGS = ("sync", "gpsimd", "vector", "scalar", "tensor")


class Prog:

    def __init__(self, nc):
        self.nc = nc
        self.streams = {e: [] for e in ENGS}
        self.cnt = {}
        self.sems = {}
        self.marks = {}

    def mark(self, label):
        self.marks[label] = {e: len(self.streams[e]) for e in ENGS}

    def truncate_to(self, label):
        cut = self.marks[label]
        self.streams = {e: self.streams[e][:cut[e]] for e in ENGS}

    def sem(self, name):
        s = self.nc.alloc_semaphore(name)
        self.cnt[name] = 0
        self.sems[name] = s
        return s

    def emit(self, eng, fn):
        self.streams[eng].append(fn)

    def dma(self, eng, out, in_, q=None):
        name = q or {"sync": "D", "gpsimd": "Dg", "scalar": "Dsc"}[eng]
        s = self.sems[name]
        self.cnt[name] += 16
        self.emit(eng, lambda en, out=out, in_=in_, s=s: en.dma_start(
            out=out, in_=in_).then_inc(s, 16))
        return self.cnt[name]

    def wait(self, eng, name, val=None):
        s = self.sems[name]
        v = self.cnt[name] if val is None else val
        self.emit(eng, lambda en, s=s, v=v: en.wait_ge(s, v))

    def op(self, eng, fn, name=None):
        name = name or {"vector": "V", "scalar": "S", "tensor": "T",
                        "gpsimd": "G"}[eng]
        s = self.sems[name]
        self.cnt[name] += 1
        self.emit(eng, lambda en, fn=fn, s=s: fn(en).then_inc(s, 1))
        return self.cnt[name]

    def raw(self, eng, fn):
        self.emit(eng, fn)

    def finish(self, block):
        for eng in ENGS:
            fns = self.streams[eng]

            def body(en, fns=fns):
                for fn in fns:
                    fn(en)

            getattr(block, eng)(body)


def _emit_quant_probe(p, bc, bcs, bexp, bchi, bclo, pf, pi16):
    p.op("vector", lambda v: v.memset(pf, 0.75))
    p.wait("vector", "V")
    p.op("vector", lambda v: v.tensor_scalar(
        out=pi16, in0=pf, scalar1=16384.0, scalar2=None, op0=ALU.add))
    p.wait("vector", "V")
    p.op("vector", lambda v: v.tensor_scalar(
        out=bc, in0=pi16, scalar1=16384.0, scalar2=None, op0=ALU.subtract))
    p.wait("vector", "V")
    p.op("vector", lambda v: v.tensor_scalar(
        out=bc, in0=bc, scalar1=-16384.5, scalar2=16384.5,
        op0=ALU.mult, op1=ALU.add))
    p.wait("vector", "V")
    p.op("vector", lambda v: v.tensor_scalar(
        out=bcs, in0=bc, scalar1=0.5, scalar2=0.0,
        op0=ALU.subtract, op1=ALU.max))
    p.wait("vector", "V")
    p.op("vector", lambda v: v.tensor_scalar(
        out=bexp, in0=bcs, scalar1=-1.0 / 16.0, scalar2=None, op0=ALU.mult))
    p.op("vector", lambda v: v.tensor_scalar(
        out=bchi, in0=bcs, scalar1=127.0, scalar2=None, op0=ALU.add))
    p.op("vector", lambda v: v.tensor_scalar(
        out=bclo, in0=bcs, scalar1=-128.0, scalar2=None, op0=ALU.add))
    p.wait("vector", "V")


def _emit_ln(p, src, bc_or_none, out_tile, scr):
    msum, ssum, mmean, rstd, sq_scr = (scr[k] for k in
                                       ("msum", "ssum", "mmean", "rstd",
                                        "sq_scr"))
    p.wait("vector", "V")
    p.wait("scalar", "V")
    p.wait("scalar", "S")
    p.op("vector", lambda v: v.reduce_sum(out=msum, in_=src, axis=AXX))
    p.op("scalar", lambda s: s.activation(
        out=sq_scr, in_=src, func=AF.Square, bias=0.0, scale=1.0,
        accum_out=ssum))
    p.wait("vector", "V")
    p.wait("vector", "S")
    p.op("vector", lambda v: v.tensor_scalar(
        out=mmean, in0=msum, scalar1=1.0 / C, scalar2=None, op0=ALU.mult))
    p.wait("vector", "V")
    p.op("vector", lambda v: v.scalar_tensor_tensor(
        out=rstd, in0=mmean, scalar=-1.0, in1=mmean,
        op0=ALU.mult, op1=ALU.mult))
    p.wait("vector", "V")
    p.op("vector", lambda v: v.scalar_tensor_tensor(
        out=rstd, in0=ssum, scalar=1.0 / C, in1=rstd,
        op0=ALU.mult, op1=ALU.add))
    p.wait("vector", "V")
    p.op("vector", lambda v: v.tensor_scalar(
        out=rstd, in0=rstd, scalar1=EPS, scalar2=None, op0=ALU.add))
    p.wait("scalar", "V")
    p.op("scalar", lambda s: s.activation(
        out=rstd, in_=rstd, func=AF.Sqrt, bias=0.0, scale=1.0))
    p.wait("vector", "S")
    p.op("vector", lambda v: v.reciprocal(out=rstd, in_=rstd))
    p.wait("vector", "V")
    if bc_or_none is not None:
        p.op("vector", lambda v: v.tensor_scalar(
            out=rstd, in0=rstd, scalar1=16.0, scalar2=None, op0=ALU.mult))
        p.wait("vector", "V")
        p.op("vector", lambda v: v.tensor_scalar(
            out=scr["tnorm"], in0=src, scalar1=mmean, scalar2=rstd,
            op0=ALU.subtract, op1=ALU.mult))
        p.wait("vector", "V")
        p.op("vector", lambda v: v.tensor_scalar(
            out=out_tile, in0=scr["tnorm"], scalar1=bc_or_none, scalar2=None,
            op0=ALU.add))
    else:
        p.op("vector", lambda v: v.tensor_scalar(
            out=out_tile, in0=src, scalar1=mmean, scalar2=rstd,
            op0=ALU.subtract, op1=ALU.mult))
    p.wait("vector", "V")


def build_launch_a(dbg=False):
    nc = bass.Bass("TRN2", target_bir_lowering=False)

    x_in = nc.dram_tensor("x", [N, C], F32, kind="ExternalInput").ap()
    wqT = nc.dram_tensor("wqT", [C, C], F32, kind="ExternalInput").ap()
    wkvT = nc.dram_tensor("wkvT", [C, 2 * C], F32, kind="ExternalInput").ap()
    wprT = nc.dram_tensor("wprT", [C, C], F32, kind="ExternalInput").ap()
    bpr_in = nc.dram_tensor("bpr", [128, 8], F32, kind="ExternalInput").ap()
    wr_in = nc.dram_tensor("wr", [C, E], F32, kind="ExternalInput").ap()
    ident_in = nc.dram_tensor("ident", [128, 128], F32, kind="ExternalInput").ap()
    w1_in = nc.dram_tensor("w1e", [C, HID], F32, kind="ExternalInput").ap()
    w2_in = nc.dram_tensor("w2e", [HID, C], F32, kind="ExternalInput").ap()

    x1_out = nc.dram_tensor("x1", [N, C], F32, kind="ExternalOutput").ap()
    x2n_out = nc.dram_tensor("x2n", [N, C], BF16, kind="ExternalOutput").ap()
    meta_out = nc.dram_tensor("meta", [128, 4 * NTILE], F32,
                              kind="ExternalOutput").ap()
    w1b_out = nc.dram_tensor("w1b", [C, HID], BF16, kind="ExternalOutput").ap()
    w2b_out = nc.dram_tensor("w2b", [HID, C], BF16, kind="ExternalOutput").ap()

    _ik = "ExternalOutput" if dbg else "Internal"
    qop_d = nc.dram_tensor("qop_d", [C, N], BF16, kind=_ik).ap()
    kop_d = nc.dram_tensor("kop_d", [C, N], BF16, kind=_ik).ap()
    zd = nc.dram_tensor("zd", [C, N], BF16, kind=_ik).ap()
    if dbg:
        wqs_d = nc.dram_tensor("wqs_d", [128, 8 * C], BF16,
                               kind="ExternalOutput").ap()
        xqfm_d = nc.dram_tensor("xqfm_d", [128, 8 * N], BF16,
                                kind="ExternalOutput").ap()

    p = Prog(nc)
    for s in ("D", "Dg", "Dsc", "T", "V", "S", "G", "DgA", "DgB", "Dz"):
        p.sem(s)

    ctx = ExitStack()
    with ctx:
        en = ctx.enter_context

        def sb(name, shape, dt):
            return en(nc.sbuf_tensor(name, shape, dt)).ap()

        _ovl = {"base": None, "cur": {}}

        def sb_at(region, name, shape, dt):
            if _ovl["base"] is None:
                _ovl["base"] = (nc.sbuf_base + 31) // 32 * 32
            cur = _ovl["cur"].get(region, 0)
            per_part = int(np.prod(shape[1:])) * mybir.dt.size(dt)
            per_part = (per_part + 31) // 32 * 32
            t = nc.alloc_sbuf_tensor_at(name, shape, dt,
                                        offset=_ovl["base"] + cur)
            _ovl["cur"][region] = cur + per_part
            return t.ap()

        ident = sb("identb", [128, 128], BF16)
        identf = sb("identf", [128, 128], F32)
        bc = sb("bc", [128, 1], F32)
        bcs = sb("bcs", [128, 1], F32)
        bexp = sb("bexp", [128, 1], F32)
        bchi = sb("bchi", [128, 1], F32)
        bclo = sb("bclo", [128, 1], F32)
        pf = sb("pf", [128, 1], F32)
        pi16 = sb("pi16", [128, 1], I16)
        bprj = sb("bprj", [128, 8], F32)
        onesrow = sb("onesrow", [64, 128], BF16)
        ones1 = sb("ones1", [128, 1], BF16)

        xt = sb("xt", [128, C], F32)
        sq_scr = sb("sq_scr", [128, C], BF16)
        scr = {
            "msum": sb("msum", [128, 1], F32),
            "ssum": sb("ssum", [128, 1], F32),
            "mmean": sb("mmean", [128, 1], F32),
            "rstd": sb("rstd", [128, 1], F32),
            "sq_scr": sq_scr,
            "tnorm": sb("tnorm", [128, C], F32),
        }
        xq_i16 = sb("xq_i16", [128, C], I16)
        xq_bf = sb("xq_bf", [128, C], BF16)

        stg16 = [sb(f"stg16_{i}", [128, 512], I16) for i in range(2)]
        stgbf = [sb(f"stgbf_{i}", [128, 512], BF16) for i in range(2)]

        v_all = sb("v_all", [128, H * KC * DH], BF16)

        wr_s = sb("wr_s", [128, 8 * 8], BF16)
        wrf = sb("wrf", [128, 8], F32)
        cneg1024 = sb("cneg1024", [128, 1], F32)
        nc.const_aps.register(F32, -1024.0, cneg1024)

        xqfm = sb_at("A", "xqfm", [128, 8 * N], BF16)
        wq_s = sb_at("A", "wq_s", [128, 8 * C], BF16)
        wkv_s = sb_at("A", "wkv_s", [128, 8 * 2 * C], BF16)
        wtmpA = sb_at("W", "wtmpA", [128, C], F32)
        wtmp16A = sb_at("W", "wtmp16A", [128, C], I16)

        kdup = sb_at("B", "kdup", [128, N], BF16)
        qdup = sb_at("B", "qdup", [128, N], BF16)
        qs_slab = sb_at("B", "qs_slab", [128, KC * QCP], I16)
        e_slab = sb_at("B", "e_slab", [128, KC * QCP], BF16)
        u_scr = sb_at("B", "u_scr", [128, QCP], F32)
        u16_scr = sb_at("B", "u16_scr", [128, QCP], I16)
        r16 = sb_at("B", "r16", [64, QCP], F32)
        r16c = sb_at("B", "r16c", [64, QCP], BF16)
        r16b = sb_at("B", "r16b", [128, QCP], BF16)
        z_pair = sb_at("B", "z_pair", [128, N], I16)
        z_bf = sb_at("B", "z_bf", [128, N], BF16)

        zf = sb_at("C", "zf", [128, 8 * 512], BF16)
        ao_fm = sb_at("C", "ao_fm", [128, 8 * 512], F32)
        wpr_s = sb_at("C", "wpr_s", [128, 8 * C], BF16)
        wtmpC = sb_at("C", "wtmpC", [128, C], F32)
        wtmp16C = sb_at("C", "wtmp16C", [128, C], I16)
        x1t = sb_at("C", "x1t", [128, C], F32)
        x2nt = sb_at("C", "x2nt", [128, C], BF16)
        x2f_t = sb_at("C", "x2f_t", [128, 128], BF16)
        meta_sb = sb_at("C", "meta_sb", [128, 4 * NTILE], F32)
        lgt = sb_at("C", "lgt", [128, 8], F32)
        lmax = sb_at("C", "lmax", [128, 8], F32)
        lidx = sb_at("C", "lidx", [128, 8], U32)
        lidxf = sb_at("C", "lidxf", [128, 8], F32)
        gtmp = sb_at("C", "gtmp", [128, 1], F32)

        pbA = en(nc.psum_tensor("pbA", [128, 1024], F32)).ap()
        pbB = en(nc.psum_tensor("pbB", [128, 1024], F32)).ap()
        pbC = en(nc.psum_tensor("pbC", [128, 1024], BF16)).ap()
        pbD = en(nc.psum_tensor("pbD", [128, 512], F32)).ap()
        pbE = en(nc.psum_tensor("pbE", [128, 512], F32)).ap()
        pbF = en(nc.psum_tensor("pbF", [128, 512], F32)).ap()
        ps_sc = pbA
        ps_tr = pbC[:, 0:128]
        ps_trf = pbD[:, 0:128]
        ps_mm = [pbB[:, 0:512], pbB[:, 512:1024]]
        ps_d0 = pbB[0:1, 0:QH]
        ps_d1 = pbB[32:33, 512:512 + QH]
        ps_r = pbD[:, 0:QCP]
        ps_z0 = pbE[0:64, 0:QH]
        ps_z1 = pbF[64:128, 0:QH]
        ps_lg = pbD[:, 128:136]

        blk = en(nc.Block())

        p.dma("sync", identf, ident_in)
        p.dma("sync", bprj, bpr_in)
        p.wait("vector", "D")
        p.op("vector", lambda v: v.tensor_copy(out=ident, in_=identf))
        p.op("vector", lambda v: v.memset(ones1, 1.0))
        p.op("vector", lambda v: v.memset(onesrow, 1.0))
        p.op("vector", lambda v: v.memset(cneg1024, -1024.0))
        _emit_quant_probe(p, bc, bcs, bexp, bchi, bclo, pf, pi16)
        p.wait("scalar", "V")
        p.wait("tensor", "V")
        p.wait("gpsimd", "V")

        p.dma("gpsimd", w1b_out, w1_in)
        p.dma("gpsimd", w2b_out, w2_in)
        p.mark("p0")

        def prep_w(srcT, slab, outw, wtmp, wtmp16):
            for cc in range(8):
                for cw in range(0, outw, C):
                    w = min(C, outw - cw)
                    p.wait("sync", "V")
                    dv = p.dma("sync", wtmp[:, 0:w],
                               srcT[cc * 128:(cc + 1) * 128, cw:cw + w])
                    p.wait("vector", "D", dv)
                    p.op("vector", lambda v, w=w: v.tensor_scalar(
                        out=wtmp16[:, 0:w], in0=wtmp[:, 0:w],
                        scalar1=16.0, scalar2=bc, op0=ALU.mult, op1=ALU.add))
                    p.wait("vector", "V")
                    p.op("vector", lambda v, cc=cc, cw=cw, w=w, slab=slab,
                         outw=outw: v.tensor_scalar(
                             out=slab[:, cc * outw + cw: cc * outw + cw + w],
                             in0=wtmp16[:, 0:w],
                             scalar1=bcs, scalar2=1.0 / 16.0,
                             op0=ALU.subtract, op1=ALU.mult))
                    p.wait("vector", "V")

        prep_w(wqT, wq_s, C, wtmpA, wtmp16A)
        if dbg:
            p.wait("sync", "V")
            p.dma("sync", wqs_d, wq_s)
        p.mark("p0b1")
        prep_w(wkvT, wkv_s, 2 * C, wtmpA, wtmp16A)
        p.mark("p0b")
        for cc in range(8):
            p.wait("sync", "V")
            dv = p.dma("sync", wrf, wr_in[cc * 128:(cc + 1) * 128, :])
            p.wait("vector", "D", dv)
            p.op("vector", lambda v, cc=cc: v.tensor_copy(
                out=wr_s[:, cc * 8:(cc + 1) * 8], in_=wrf))
            p.wait("vector", "V")

        for t in range(NTILE):
            p.wait("sync", "V")
            p.wait("sync", "S")
            dv = p.dma("sync", xt, x_in[t * 128:(t + 1) * 128, :])
            p.wait("vector", "D", dv)
            p.wait("scalar", "D", dv)
            _emit_ln(p, xt, bc, xq_i16, scr)
            p.op("vector", lambda v: v.tensor_scalar(
                out=xq_bf, in0=xq_i16, scalar1=bcs, scalar2=None,
                op0=ALU.subtract))
            p.wait("tensor", "V")
            for fc in range(8):
                tv = p.op("tensor", lambda t_, fc=fc: t_.transpose(
                    ps_tr, xq_bf[:, fc * 128:(fc + 1) * 128], ident))
                p.wait("vector", "T", tv)
                p.op("vector", lambda v, fc=fc, t=t: v.tensor_copy(
                    out=xqfm[:, fc * N + t * 128: fc * N + (t + 1) * 128],
                    in_=ps_tr))
                p.wait("tensor", "V")

        if dbg:
            p.wait("sync", "V")
            p.wait("sync", "T")
            p.dma("sync", xqfm_d, xqfm)
        p.mark("p1")
        slot_dg = [0, 0]
        slot_sem = ["DgA", "DgB"]
        slot_k1 = [0, 0]
        it = 0
        for kind, dst in (("q", qop_d), ("k", kop_d)):
            for oc in range(8):
                for tc in range(NQCP):
                    slot = it % 2
                    pm = ps_mm[slot]
                    p.wait("tensor", "V", slot_k1[slot])
                    for cc in range(8):
                        if kind == "q":
                            w_ap = wq_s[:, cc * C + oc * 128:
                                        cc * C + oc * 128 + 128]
                        else:
                            w_ap = wkv_s[:, cc * 2 * C + oc * 128:
                                         cc * 2 * C + oc * 128 + 128]
                        fn = lambda t_, w_ap=w_ap, cc=cc, pm=pm, tc=tc: \
                            t_.matmul(
                                pm, w_ap,
                                xqfm[:, cc * N + tc * 512:
                                     cc * N + (tc + 1) * 512],
                                start=(cc == 0), stop=(cc == 7))
                        (p.op if cc == 7 else p.raw)("tensor", fn)
                    p.wait("vector", "T")
                    p.wait("vector", slot_sem[slot], slot_dg[slot])
                    p.op("vector", lambda v, pm=pm, slot=slot: v.tensor_scalar(
                        out=stg16[slot], in0=pm, scalar1=bc, scalar2=None,
                        op0=ALU.add))
                    p.wait("vector", "V")
                    if kind == "q":
                        k1v = p.op("vector", lambda v, slot=slot:
                                   v.tensor_scalar(
                                       out=stgbf[slot], in0=stg16[slot],
                                       scalar1=bcs, scalar2=2.0 ** -7,
                                       op0=ALU.subtract, op1=ALU.mult))
                    else:
                        k1v = p.op("vector", lambda v, slot=slot:
                                   v.tensor_scalar(
                                       out=stgbf[slot], in0=stg16[slot],
                                       scalar1=bcs, scalar2=None,
                                       op0=ALU.subtract))
                    slot_k1[slot] = k1v
                    p.wait("gpsimd", "V", k1v)
                    slot_dg[slot] = p.dma(
                        "gpsimd",
                        dst[oc * 128:(oc + 1) * 128, tc * 512:(tc + 1) * 512],
                        stgbf[slot], q=slot_sem[slot])
                    it += 1

        p.mark("p2")
        for tb in range(NTILE):
            for voc in range(2):
                slot = it % 2
                pm = ps_mm[slot]
                p.wait("tensor", "V", slot_k1[slot])
                for cc in range(8):
                    fn = lambda t_, tb=tb, voc=voc, cc=cc, pm=pm: t_.matmul(
                        pm,
                        xqfm[:, cc * N + tb * 128: cc * N + (tb + 1) * 128],
                        wkv_s[:, cc * 2 * C + C + voc * 512:
                              cc * 2 * C + C + (voc + 1) * 512],
                        start=(cc == 0), stop=(cc == 7))
                    (p.op if cc == 7 else p.raw)("tensor", fn)
                p.wait("vector", "T")
                p.op("vector", lambda v, pm=pm, slot=slot: v.tensor_scalar(
                    out=stg16[slot], in0=pm, scalar1=bc, scalar2=None,
                    op0=ALU.add))
                p.wait("vector", "V")
                dst = v_all.rearrange("q (hh k d) -> q hh k d",
                                      hh=H, k=KC, d=DH)[
                    :, voc * 8:(voc + 1) * 8, tb:tb + 1, :]
                src = stg16[slot].rearrange("q (hh o d) -> q hh o d",
                                            hh=8, o=1, d=DH)
                k1v = p.op("vector", lambda v, dst=dst, src=src:
                           v.tensor_scalar(
                               out=dst, in0=src, scalar1=bcs,
                               scalar2=None, op0=ALU.subtract))
                slot_k1[slot] = k1v
                it += 1
        p.wait("tensor", "V")

        p.mark("p2b")
        vh = v_all.rearrange("q (hh k d) -> q hh k d", hh=H, k=KC, d=DH)
        p.wait("sync", "DgA")
        p.wait("sync", "DgB")
        last_z_dg = 0
        for h in range(H):
            p.wait("sync", "T")
            p.wait("scalar", "V")
            p.wait("vector", "V")
            p.wait("scalar", "S")
            for half in range(2):
                p.dma("sync", kdup[64 * half:64 * half + 64, :],
                      kop_d[h * 64:(h + 1) * 64, :])
                p.dma("sync", qdup[64 * half:64 * half + 64, :],
                      qop_d[h * 64:(h + 1) * 64, :])
            p.wait("tensor", "D")
            if h == 0:
                p.mark("hload")
            for qc in range(NQCP):
                q0 = qc * QCP
                for kc in range(KC):
                    slot = kc % 2
                    pm0 = ps_sc[:, slot * QH:(slot + 1) * QH]
                    pm1 = ps_sc[:, 512 + slot * QH:512 + (slot + 1) * QH]
                    pmv = ps_sc.rearrange("p (b c) -> p b c", b=2)[
                        :, :, slot * QH:(slot + 1) * QH]
                    if kc >= 2:
                        p.wait("tensor", "V")
                        p.wait("tensor", "S")
                    p.raw("tensor", lambda t_, kc=kc, pm0=pm0, q0=q0:
                          t_.matmul(
                              pm0, kdup[0:64, kc * 128:(kc + 1) * 128],
                              qdup[0:64, q0:q0 + QH], start=True, stop=True,
                              tile_position=(0, 0)))
                    tv = p.op("tensor", lambda t_, kc=kc, pm1=pm1, q0=q0:
                              t_.matmul(
                                  pm1,
                                  kdup[64:128, kc * 128:(kc + 1) * 128],
                                  qdup[64:128, q0 + QH:q0 + QCP],
                                  start=True, stop=True,
                                  tile_position=(64, 0)))
                    dstq = qs_slab[:, kc * QCP:(kc + 1) * QCP].rearrange(
                        "p (b c) -> p b c", b=2)
                    p.wait("vector", "T", tv)
                    p.op("vector", lambda v, pmv=pmv, dstq=dstq:
                         v.tensor_scalar(out=dstq, in0=pmv, scalar1=bc,
                                         scalar2=None, op0=ALU.add))
                    if h == 0 and qc == 0:
                        p.mark(f"sc{kc}")
                p.wait("scalar", "V")
                p.wait("scalar", "S")
                ev = p.op("scalar", lambda s: s.activation(
                    out=e_slab, in_=qs_slab, func=AF.Exp, bias=bexp,
                    scale=1.0 / 16.0))
                p.wait("vector", "S", ev)
                ecv = p.op("vector", lambda v: v.tensor_scalar(
                    out=e_slab, in0=e_slab, scalar1=EHI, scalar2=ELO,
                    op0=ALU.min, op1=ALU.max))
                p.wait("tensor", "V", ecv)
                if h == 0 and qc == 0:
                    p.mark("exp")
                for kc in range(KC):
                    for half in range(2):
                        pd = ps_d0 if half == 0 else ps_d1
                        fn = lambda t_, kc=kc, half=half, pd=pd: t_.matmul(
                            pd, ones1,
                            e_slab[:, kc * QCP + half * QH:
                                   kc * QCP + (half + 1) * QH],
                            start=(kc == 0), stop=(kc == KC - 1),
                            tile_position=(0, 32 * half))
                        (p.op if (kc == KC - 1 and half == 1) else p.raw)(
                            "tensor", fn)
                p.wait("vector", "T")
                if h == 0 and qc == 0:
                    p.mark("den")
                for half in range(2):
                    pd = ps_d0 if half == 0 else ps_d1
                    p.op("vector", lambda v, half=half, pd=pd: v.reciprocal(
                        out=r16[32 * half:32 * half + 1, 0:QH], in_=pd))
                    p.wait("vector", "V")
                    p.op("vector", lambda v, half=half: v.tensor_scalar(
                        out=r16c[32 * half:32 * half + 1, 0:QH],
                        in0=r16[32 * half:32 * half + 1, 0:QH],
                        scalar1=16.0, scalar2=None, op0=ALU.mult))
                    p.wait("tensor", "V")
                for half in range(2):
                    dst_r = (ps_r[:, 0:QH] if half == 0
                             else pbE[:, QH:QCP])
                    fn = lambda t_, half=half, dst_r=dst_r: t_.matmul(
                        dst_r,
                        onesrow[32 * half:32 * half + 1, :],
                        r16c[32 * half:32 * half + 1, 0:QH],
                        start=True, stop=True)
                    (p.op if half == 1 else p.raw)("tensor", fn)
                p.wait("vector", "T")
                if h == 0 and qc == 0:
                    p.mark("bcast")
                p.raw("vector", lambda v: v.tensor_copy(
                    out=r16b[:, 0:QH], in_=ps_r[:, 0:QH]))
                rbv = p.op("vector", lambda v: v.tensor_copy(
                    out=r16b[:, QH:QCP], in_=pbE[:, QH:QCP]))
                for kc in range(KC):
                    esl = e_slab[:, kc * QCP:(kc + 1) * QCP]
                    asl = e_slab[:, kc * QCP:(kc + 1) * QCP]
                    p.wait("vector", "V")
                    p.op("vector", lambda v, esl=esl: v.tensor_tensor(
                        out=u_scr, in0=esl, in1=r16b, op=ALU.mult))
                    p.wait("vector", "V")
                    p.op("vector", lambda v: v.tensor_scalar(
                        out=u16_scr, in0=u_scr, scalar1=bc, scalar2=None,
                        op0=ALU.add))
                    p.wait("vector", "V")
                    p.op("vector", lambda v, asl=asl: v.tensor_scalar(
                        out=asl, in0=u16_scr, scalar1=bcs, scalar2=None,
                        op0=ALU.subtract))
                p.wait("tensor", "V")
                p.wait("tensor", "S")
                if h == 0 and qc == 0:
                    p.mark("q2")
                for kc in range(KC):
                    for half in range(2):
                        pz = ps_z0 if half == 0 else ps_z1
                        fn = lambda t_, kc=kc, half=half, h=h, pz=pz: t_.matmul(
                            pz, vh[:, h, kc, :],
                            e_slab[:, kc * QCP + half * QH:
                                   kc * QCP + (half + 1) * QH],
                            start=(kc == 0), stop=(kc == KC - 1),
                            tile_position=(0, 64 * half))
                        (p.op if (kc == KC - 1 and half == 1) else p.raw)(
                            "tensor", fn)
                p.wait("vector", "T")
                if h == 0 and qc == 0:
                    p.mark("av")
                if h % 2 == 0 and qc == 0:
                    p.wait("vector", "Dz", last_z_dg)
                for half in range(2):
                    src = ps_z0 if half == 0 else ps_z1
                    drow = 64 * (h % 2)
                    dst16 = z_pair[drow:drow + 64,
                                   q0 + half * QH: q0 + (half + 1) * QH]
                    bslice = bc[0:64, :] if drow == 0 else bc[64:128, :]
                    p.op("vector", lambda v, src=src, dst16=dst16,
                         bslice=bslice: v.tensor_scalar(
                             out=dst16, in0=src, scalar1=1.0 / 16.0,
                             scalar2=bslice, op0=ALU.mult, op1=ALU.add))
            if h % 2 == 1:
                p.wait("vector", "V")
                zclip = qs_slab[:, 0:N]
                p.op("vector", lambda v, zclip=zclip: v.tensor_scalar(
                    out=zclip, in0=z_pair, scalar1=bchi,
                    scalar2=bclo, op0=ALU.min, op1=ALU.max))
                p.wait("vector", "V")
                zbv = p.op("vector", lambda v, zclip=zclip: v.tensor_scalar(
                    out=z_bf, in0=zclip, scalar1=bcs, scalar2=None,
                    op0=ALU.subtract))
                p.wait("gpsimd", "V", zbv)
                last_z_dg = p.dma(
                    "gpsimd", zd[(h // 2) * 128:(h // 2) * 128 + 128, :], z_bf,
                    q="Dz")
            p.mark(f"h{h}")

        p.mark("attn")
        p.wait("sync", "Dz")
        p.wait("sync", "V")
        p.wait("sync", "T")
        prep_w(wprT, wpr_s, C, wtmpC, wtmp16C)
        for tc in range(NQCP):
            p.wait("sync", "T")
            p.wait("sync", "V")
            for cc in range(8):
                p.dma("sync", zf[:, cc * 512:(cc + 1) * 512],
                      zd[cc * 128:(cc + 1) * 128, tc * 512:(tc + 1) * 512])
            p.wait("tensor", "D")
            for oc in range(8):
                slot = oc % 2
                pm = ps_mm[slot]
                p.wait("tensor", "V")
                for cc in range(8):
                    fn = lambda t_, oc=oc, cc=cc, pm=pm: t_.matmul(
                        pm, wpr_s[:, cc * C + oc * 128: cc * C + oc * 128 + 128],
                        zf[:, cc * 512:(cc + 1) * 512],
                        start=(cc == 0), stop=(cc == 7))
                    (p.op if cc == 7 else p.raw)("tensor", fn)
                p.wait("vector", "T")
                p.op("vector", lambda v, pm=pm, oc=oc: v.tensor_scalar(
                    out=ao_fm[:, oc * 512:(oc + 1) * 512], in0=pm,
                    scalar1=1.0 / 16.0, scalar2=bprj[:, oc:oc + 1],
                    op0=ALU.mult, op1=ALU.add))
            p.wait("vector", "V")
            for tt in range(4):
                t_g = tc * 4 + tt
                p.wait("scalar", "V")
                dv = p.dma("scalar", xt, x_in[t_g * 128:(t_g + 1) * 128, :])
                p.wait("vector", "Dsc", dv)
                p.wait("vector", "Dg")
                p.wait("tensor", "V")
                for oc in range(8):
                    tv = p.op("tensor", lambda t_, oc=oc, tt=tt: t_.transpose(
                        ps_trf,
                        ao_fm[:, oc * 512 + tt * 128:
                              oc * 512 + (tt + 1) * 128],
                        identf))
                    p.wait("vector", "T", tv)
                    p.op("vector", lambda v, oc=oc: v.scalar_tensor_tensor(
                        out=x1t[:, oc * 128:(oc + 1) * 128], in0=ps_trf,
                        scalar=0.0, in1=xt[:, oc * 128:(oc + 1) * 128],
                        op0=ALU.bypass, op1=ALU.add))
                    p.wait("tensor", "V")
                p.wait("gpsimd", "V")
                p.dma("gpsimd", x1_out[t_g * 128:(t_g + 1) * 128, :], x1t)
                p.wait("scalar", "V")
                _emit_ln(p, x1t, None, x2nt, scr)
                p.wait("gpsimd", "V")
                p.dma("gpsimd", x2n_out[t_g * 128:(t_g + 1) * 128, :], x2nt)
                p.wait("tensor", "V")
                for cc in range(8):
                    tv = p.op("tensor", lambda t_, cc=cc: t_.transpose(
                        ps_tr, x2nt[:, cc * 128:(cc + 1) * 128], ident))
                    p.wait("vector", "T")
                    xv = p.op("vector", lambda v: v.tensor_copy(
                        out=x2f_t, in_=ps_tr))
                    p.wait("tensor", "V", xv)
                    fn = lambda t_, cc=cc: t_.matmul(
                        ps_lg, x2f_t, wr_s[:, cc * 8:(cc + 1) * 8],
                        start=(cc == 0), stop=(cc == 7))
                    p.op("tensor", fn)
                p.wait("vector", "T")
                p.op("vector", lambda v: v.tensor_copy(out=lgt, in_=ps_lg))
                p.wait("vector", "V")
                p.op("vector", lambda v: v.max(out=lmax, in_=lgt))
                p.wait("vector", "V")
                p.op("vector", lambda v: v.max_index(
                    out=lidx, in_max=lmax, in_values=lgt))
                p.wait("vector", "V")
                p.op("vector", lambda v: v.tensor_copy(out=lidxf, in_=lidx))
                p.wait("vector", "V")
                p.op("vector", lambda v: v.tensor_sub(
                    out=gtmp, in0=lmax[:, 0:1], in1=lmax[:, 1:2]))
                p.wait("scalar", "V")
                sgv = p.op("scalar", lambda s: s.activation(
                    out=gtmp, in_=gtmp, func=AF.Sigmoid, bias=0.0, scale=1.0))
                p.wait("vector", "S", sgv)
                p.op("vector", lambda v, t_g=t_g: v.tensor_copy(
                    out=meta_sb[:, 4 * t_g + 0:4 * t_g + 1],
                    in_=lidxf[:, 0:1]))
                p.op("vector", lambda v, t_g=t_g: v.tensor_copy(
                    out=meta_sb[:, 4 * t_g + 1:4 * t_g + 2],
                    in_=lidxf[:, 1:2]))
                p.op("vector", lambda v, t_g=t_g: v.tensor_copy(
                    out=meta_sb[:, 4 * t_g + 2:4 * t_g + 3], in_=gtmp))
                p.op("vector", lambda v, t_g=t_g: v.tensor_scalar(
                    out=meta_sb[:, 4 * t_g + 3:4 * t_g + 4], in0=gtmp,
                    scalar1=-1.0, scalar2=1.0, op0=ALU.mult, op1=ALU.add))
                p.wait("vector", "V")

        p.wait("gpsimd", "V")
        p.dma("gpsimd", meta_out, meta_sb)
        p.wait("gpsimd", "Dg")
        p.wait("gpsimd", "D")

        p.finish(blk)

    nc._prog_cnt = dict(p.cnt)
    return nc


def _host_prep_a(inputs):
    x = np.asarray(inputs["L0_attn_x"])
    wq = np.asarray(inputs["L0_attn_Wq"])
    wkv = np.asarray(inputs["L0_attn_Wkv"])
    wpr = np.asarray(inputs["L0_attn_Wproj"])
    bpr = np.asarray(inputs["L0_attn_bproj"])
    wr = np.asarray(inputs["L0_ffn_w_router"])
    w1 = np.asarray(inputs["L0_ffn_w1"])
    w2 = np.asarray(inputs["L0_ffn_w2"])
    wqT = np.ascontiguousarray(wq.T)
    wkvT = np.ascontiguousarray(wkv.T)
    wprT = np.ascontiguousarray(wpr.T)
    bprT = np.ascontiguousarray(bpr.reshape(8, 128).T)
    ident = np.eye(128, dtype=np.float32)
    maps = []
    for c in range(8):
        maps.append({
            "x": np.ascontiguousarray(x[c]),
            "wqT": wqT, "wkvT": wkvT, "wprT": wprT, "bpr": bprT,
            "wr": np.ascontiguousarray(wr),
            "ident": ident,
            "w1e": np.ascontiguousarray(w1[c]),
            "w2e": np.ascontiguousarray(w2[c]),
        })
    return maps


def kernel(**inputs):
    nc_a = build_launch_a()
    maps_a = _host_prep_a(inputs)
    res_a = run_bass_kernel_spmd(nc_a, maps_a, core_ids=list(range(8)))
    raise NotImplementedError("launch B pending")



JW = 512
GCOLS = JW * E
BUFR = 8 * 1024 + 128
HUGE = float(1 << 20)
NCHUNK = 4


def build_launch_b0(dbg=False):
    nc = bass.Bass("TRN2", target_bir_lowering=False)

    easg_in = nc.dram_tensor("easg", [128, JW], F32, kind="ExternalInput").ap()
    tril_in = nc.dram_tensor("tril", [128, 128], F32,
                             kind="ExternalInput").ap()
    trilmy_in = nc.dram_tensor("trilmy", [128, 128], F32,
                               kind="ExternalInput").ap()
    onesmy_in = nc.dram_tensor("onesmy", [128, 1], F32,
                               kind="ExternalInput").ap()
    minemask_in = nc.dram_tensor("minemask", [128, 1], F32,
                                 kind="ExternalInput").ap()
    eoff_in = nc.dram_tensor("eoffrow", [1, GCOLS], F32,
                             kind="ExternalInput").ap()
    idxr_out = nc.dram_tensor("idxr", [128, JW], F32,
                              kind="ExternalOutput").ap()
    if dbg:
        kp_out = nc.dram_tensor("kp_d", [128, GCOLS], F32,
                                kind="ExternalOutput").ap()
        rows_out = nc.dram_tensor("rows_d", [1, 7 * GCOLS], F32,
                                  kind="ExternalOutput").ap()
        idxaf_out = nc.dram_tensor("idxaf_d", [128, GCOLS], F32,
                                   kind="ExternalOutput").ap()

    p = Prog(nc)
    for s in ("D", "T", "V", "S"):
        p.sem(s)

    ctx = ExitStack()
    with ctx:
        en = ctx.enter_context

        def sb(name, shape, dt):
            return en(nc.sbuf_tensor(name, shape, dt)).ap()

        easg_s = sb("easg_s", [128, JW], F32)
        ind3 = sb("ind3", [128, GCOLS], F32)
        kp = sb("kp", [128, GCOLS], F32)
        km = sb("km", [128, 512], F32)
        scr = sb("scr", [128, 512], F32)
        idxaf = sb("idxaf", [128, GCOLS], F32)
        idxr = sb("idxr_s", [128, JW], F32)
        rows7 = sb("rows7", [1, 7 * GCOLS], F32)

        def R(i):
            return rows7[0:1, i * GCOLS:(i + 1) * GCOLS]
        minemask = sb("minemask_s", [128, 1], F32)
        onesmy = sb("onesmy_s", [128, 1], F32)
        onescol = sb("onescol", [128, 1], F32)
        onesrow1 = sb("onesrow1", [1, 128], F32)
        tril_s = sb("tril_s", [128, 128], F32)
        trilmy_s = sb("trilmy_s", [128, 128], F32)

        pbA = en(nc.psum_tensor("pbA", [128, 512], F32)).ap()
        pbB = en(nc.psum_tensor("pbB", [128, 512], F32)).ap()
        pbC = en(nc.psum_tensor("pbC", [128, 512], F32)).ap()
        pbD = en(nc.psum_tensor("pbD", [128, 512], F32)).ap()

        blk = en(nc.Block())

        p.op("vector", lambda v: v.memset(onescol, 1.0))
        p.op("vector", lambda v: v.memset(onesrow1, 1.0))
        p.op("vector", lambda v: v.memset(rows7, 0.0))
        p.dma("sync", easg_s, easg_in)
        p.dma("sync", minemask, minemask_in)
        p.dma("sync", onesmy, onesmy_in)
        p.dma("sync", tril_s, tril_in)
        p.dma("sync", trilmy_s, trilmy_in)
        p.wait("sync", "V")
        p.dma("sync", R(4), eoff_in)
        p.wait("vector", "D")

        ind3v = ind3.rearrange("p (j e) -> p j e", e=8)
        for e in range(E):
            p.op("vector", lambda v, e=e: v.tensor_scalar(
                out=ind3v[:, :, e:e + 1],
                in0=easg_s.rearrange("p (j o) -> p j o", o=1),
                scalar1=float(e), scalar2=None, op0=ALU.is_equal))
        p.wait("vector", "V")
        p.wait("tensor", "V")

        for grp in range(8):
            ps = (pbC if grp % 2 == 0 else pbD)[0:1, :]
            tv = p.op("tensor", lambda t_, grp=grp, ps=ps: t_.matmul(
                ps, onescol, ind3[:, grp * 512:(grp + 1) * 512],
                start=True, stop=True))
            p.wait("vector", "T", tv)
            p.op("vector", lambda v, grp=grp, ps=ps: v.tensor_copy(
                out=R(0)[:, grp * 512:(grp + 1) * 512], in_=ps))
            if grp >= 1:
                p.wait("tensor", "V")

        def log_prefix(row, scratch):
            src_, dst = row, scratch
            for i in range(9):
                s8 = 8 * (1 << i)
                p.wait("vector", "V")
                p.op("vector", lambda v, src_=src_, dst=dst, s8=s8:
                     v.tensor_copy(out=R(dst)[:, 0:s8],
                                   in_=R(src_)[:, 0:s8]))
                p.op("vector", lambda v, src_=src_, dst=dst, s8=s8:
                     v.tensor_tensor(
                         out=R(dst)[:, s8:GCOLS],
                         in0=R(src_)[:, s8:GCOLS],
                         in1=R(src_)[:, 0:GCOLS - s8], op=ALU.add))
                src_, dst = dst, src_
            return src_

        rtot = log_prefix(0, 5)
        p.wait("vector", "V")
        p.op("vector", lambda v: v.memset(R(2)[:, 0:8], 0.0))
        p.op("vector", lambda v, rtot=rtot: v.tensor_copy(
            out=R(2)[:, 8:GCOLS], in_=R(rtot)[:, 0:GCOLS - 8]))
        p.wait("vector", "V")
        p.wait("tensor", "V")

        for grp in range(8):
            ps = pbA if grp % 2 == 0 else pbB
            sl = slice(grp * 512, (grp + 1) * 512)
            p.raw("tensor", lambda t_, ps=ps, sl=sl: t_.matmul(
                ps, tril_s, ind3[:, sl], start=True, stop=False))
            tv = p.op("tensor", lambda t_, ps=ps, sl=sl: t_.matmul(
                ps, onesrow1, R(2)[:, sl], start=False, stop=True))
            p.wait("vector", "T", tv)
            p.op("vector", lambda v, ps=ps: v.tensor_scalar(
                out=scr, in0=ps, scalar1=float(CAP) - 0.5, scalar2=None,
                op0=ALU.is_lt))
            p.wait("vector", "V")
            p.op("vector", lambda v, sl=sl: v.tensor_tensor(
                out=kp[:, sl], in0=scr, in1=ind3[:, sl], op=ALU.mult))
            p.wait("vector", "V")
            if grp >= 1:
                p.wait("tensor", "V")
        p.wait("vector", "V")
        p.wait("tensor", "V")

        for grp in range(8):
            ps = (pbC if grp % 2 == 0 else pbD)[0:1, :]
            tv = p.op("tensor", lambda t_, grp=grp, ps=ps: t_.matmul(
                ps, onesmy, kp[:, grp * 512:(grp + 1) * 512],
                start=True, stop=True))
            p.wait("vector", "T", tv)
            p.op("vector", lambda v, grp=grp, ps=ps: v.tensor_copy(
                out=R(1)[:, grp * 512:(grp + 1) * 512], in_=ps))
            if grp >= 1:
                p.wait("tensor", "V")

        rmy = log_prefix(1, 6)
        p.wait("vector", "V")
        p.op("vector", lambda v: v.tensor_copy(
            out=R(3)[:, 0:8], in_=R(4)[:, 0:8]))
        p.op("vector", lambda v, rmy=rmy: v.tensor_tensor(
            out=R(3)[:, 8:GCOLS], in0=R(rmy)[:, 0:GCOLS - 8],
            in1=R(4)[:, 8:GCOLS], op=ALU.add))
        p.wait("vector", "V")
        p.wait("tensor", "V")

        for grp in range(8):
            ps = pbA if grp % 2 == 0 else pbB
            sl = slice(grp * 512, (grp + 1) * 512)
            p.raw("tensor", lambda t_, ps=ps, sl=sl: t_.matmul(
                ps, trilmy_s, kp[:, sl], start=True, stop=False))
            tv = p.op("tensor", lambda t_, ps=ps, sl=sl: t_.matmul(
                ps, onesrow1, R(3)[:, sl], start=False, stop=True))
            p.op("vector", lambda v, sl=sl: v.tensor_scalar(
                out=km, in0=kp[:, sl], scalar1=minemask, scalar2=None,
                op0=ALU.mult))
            p.wait("vector", "T", tv)
            p.wait("vector", "V")
            p.op("vector", lambda v, ps=ps: v.tensor_scalar(
                out=scr, in0=ps, scalar1=HUGE, scalar2=None,
                op0=ALU.subtract))
            p.wait("vector", "V")
            p.op("vector", lambda v: v.tensor_tensor(
                out=scr, in0=scr, in1=km, op=ALU.mult))
            p.wait("vector", "V")
            p.op("vector", lambda v, sl=sl: v.tensor_scalar(
                out=idxaf[:, sl], in0=scr, scalar1=HUGE, scalar2=None,
                op0=ALU.add))
            p.wait("vector", "V")
            if grp >= 1:
                p.wait("tensor", "V")
        p.wait("vector", "V")
        p.op("vector", lambda v: v.tensor_reduce(
            out=idxr.rearrange("p (j o) -> p j o", o=1),
            in_=idxaf.rearrange("p (j e) -> p j e", e=8),
            axis=AXX, op=ALU.min))
        p.wait("vector", "V")
        p.wait("sync", "V")
        p.dma("sync", idxr_out, idxr)
        if dbg:
            p.dma("sync", kp_out, kp)
            p.dma("sync", rows_out, rows7)
            p.dma("sync", idxaf_out, idxaf)
        p.wait("sync", "D")

        p.finish(blk)

    nc._prog_cnt = dict(p.cnt)
    return nc


def build_launch_b1():
    nc = bass.Bass("TRN2", target_bir_lowering=False)

    buf_in = nc.dram_tensor("buf", [E * 1024, C], BF16,
                            kind="ExternalInput").ap()
    gt_in = nc.dram_tensor("gt", [E * 1024, 1], F32,
                           kind="ExternalInput").ap()
    b1r = nc.dram_tensor("b1r", [E, HID], F32, kind="ExternalInput").ap()
    b2r = nc.dram_tensor("b2r", [E, C], F32, kind="ExternalInput").ap()
    w1r = [nc.dram_tensor(f"w1r{e}", [C, HID], BF16,
                          kind="ExternalInput").ap() for e in range(E)]
    w2r = [nc.dram_tensor(f"w2r{e}", [HID, C], BF16,
                          kind="ExternalInput").ap() for e in range(E)]
    ident_in = nc.dram_tensor("ident", [128, 128], F32,
                              kind="ExternalInput").ap()
    orows_out = [nc.dram_tensor(f"orows{e}", [1024, C], F32,
                                kind="ExternalOutput").ap()
                 for e in range(E)]

    p = Prog(nc)
    for s in ("D", "Dsc", "Db0", "Db1", "Dga", "Dgb", "Dw1",
              "Dw2", "Do", "Dbr", "T", "V", "S"):
        p.sem(s)

    ctx = ExitStack()
    with ctx:
        en = ctx.enter_context

        def sb(name, shape, dt):
            return en(nc.sbuf_tensor(name, shape, dt)).ap()

        rows2 = sb("rows2", [1, GCOLS + C], F32)
        onesr1 = sb("onesr1", [1, 128], F32)
        identf = sb("identf", [128, 128], F32)
        identb = sb("identb", [128, 128], BF16)
        xT_s = sb("xT_s", [128, C], BF16)
        hT_s = sb("hT_s", [128, HID], BF16)
        orows = sb("orows_s", [128, C], F32)
        g_s = [sb("g_s0", [128, 1], F32), sb("g_s1", [128, 1], F32)]
        browf = [sb("browf0", [128, C], BF16), sb("browf1", [128, C], BF16)]
        w1slab = [sb("w1slabA", [128, 16384], BF16),
                  sb("w1slabB", [128, 16384], BF16)]
        w2slab = w1slab
        h4_sb = sb("h4_sb", [128, NCHUNK * HID], BF16)

        pbA = en(nc.psum_tensor("pbA", [128, 512], F32)).ap()
        pbB = en(nc.psum_tensor("pbB", [128, 512], F32)).ap()
        pbC = en(nc.psum_tensor("pbC", [128, 512], F32)).ap()
        pbD = en(nc.psum_tensor("pbD", [128, 512], F32)).ap()
        pbTs = [en(nc.psum_tensor(f"pbT{i}", [128, 1024], BF16)).ap()
                for i in range(3)]

        blk = en(nc.Block())

        gslot_ov = [0, 0]
        p.dma("sync", identf, ident_in)
        p.op("vector", lambda v: v.memset(onesr1, 1.0))
        p.wait("vector", "D")
        p.op("vector", lambda v: v.tensor_copy(out=identb, in_=identf))
        p.wait("vector", "V")
        p.wait("sync", "V")
        p.mark("p0")

        for e in range(E):
            p.wait("sync", "S")
            p.wait("sync", "V")
            p.dma("sync", rows2[0:1, 0:GCOLS], b1r[e:e + 1, :], q="Dbr")
            brd = p.dma("sync", rows2[0:1, GCOLS:GCOLS + C],
                        b2r[e:e + 1, :], q="Dbr")
            if e > 0:
                p.wait("sync", "T")
            wd = 0
            for hh in range(2):
                for cc in range(8):
                    for g4 in range(4):
                        wd = p.dma(
                            "sync",
                            w1slab[hh][:, cc * 2048 + g4 * 512:
                                       cc * 2048 + (g4 + 1) * 512],
                            w1r[e][cc * 128:(cc + 1) * 128,
                                   hh * 2048 + g4 * 512: hh * 2048 +
                                   (g4 + 1) * 512], q="Dw1")
            w1_d = wd
            if e == 0:
                p.mark("w1l")
            for ch in range(NCHUNK):
                rb = e * 1024 + ch * 128
                bq = "Db0" if ch % 2 == 0 else "Db1"
                bv = p.dma("scalar", browf[ch % 2],
                           buf_in[rb:rb + 128, :], q=bq)
                p.wait("tensor", bq, bv)
                p.wait("tensor", "V")
                for cc in range(8):
                    pslot = pbTs[cc % 3][:, 0:128]
                    if cc >= 3:
                        p.wait("tensor", "V")
                    tv = p.op("tensor", lambda t_, cc=cc, ch=ch, pslot=pslot:
                              t_.transpose(
                                  pslot, browf[ch % 2][:, cc * 128:
                                                       (cc + 1) * 128],
                                  identb))
                    p.wait("vector", "T", tv)
                    p.op("vector", lambda v, cc=cc, pslot=pslot:
                         v.tensor_copy(out=xT_s[:, cc * 128:(cc + 1) * 128],
                                       in_=pslot))
                p.wait("tensor", "V")
                if e == 0 and ch == 0:
                    p.mark("tr0")
                if ch == 0:
                    p.wait("tensor", "Dw1", w1_d)
                    p.wait("tensor", "Dbr", brd)
                for hg in range(8):
                    ps = pbA if hg % 2 == 0 else pbB
                    if e > 0 or ch > 0 or hg >= 2:
                        p.wait("tensor", "S")
                    p.raw("tensor", lambda t_, hg=hg, ps=ps: t_.matmul(
                        ps, onesr1, rows2[0:1, hg * 512:(hg + 1) * 512],
                        start=True, stop=False))
                    for cc in range(8):
                        hh, g4 = hg // 4, hg % 4
                        fn = lambda t_, cc=cc, hh=hh, g4=g4, ps=ps: t_.matmul(
                            ps, xT_s[:, cc * 128:(cc + 1) * 128],
                            w1slab[hh][:, cc * 2048 + g4 * 512:
                                       cc * 2048 + (g4 + 1) * 512],
                            start=False, stop=(cc == 7))
                        (p.op if cc == 7 else p.raw)("tensor", fn)
                    tvh = p.cnt["T"]
                    p.wait("scalar", "T", tvh)
                    p.op("scalar", lambda s, ch=ch, hg=hg, ps=ps:
                         s.activation(
                             out=h4_sb[:, ch * HID + hg * 512:
                                       ch * HID + (hg + 1) * 512],
                             in_=ps, func=AF.Relu, bias=0.0, scale=1.0))
            if e == 0:
                p.mark("h0")
            p.wait("sync", "S")
            wd = 0
            for hh in range(2):
                for hcl in range(16):
                    wd = p.dma(
                        "sync",
                        w2slab[hh][:, hcl * 1024:(hcl + 1) * 1024],
                        w2r[e][(hh * 16 + hcl) * 128:
                               (hh * 16 + hcl + 1) * 128, :], q="Dw2")
            w2_d = wd
            for ch in range(NCHUNK):
                rb = e * 1024 + ch * 128
                if e > 0 or ch >= 2:
                    p.wait("scalar", "V", gslot_ov[ch % 2])
                gq2 = "Dga" if ch % 2 == 0 else "Dgb"
                gv = p.dma("scalar", g_s[ch % 2], gt_in[rb:rb + 128, :],
                           q=gq2)
                p.wait("tensor", "S")
                p.wait("tensor", "V")
                if ch == 0:
                    p.wait("tensor", "Dw2", w2_d)
                for hc in range(32):
                    pslot = pbTs[hc % 3][:, 0:128]
                    if hc >= 3:
                        p.wait("tensor", "V")
                    tv = p.op("tensor", lambda t_, hc=hc, ch=ch, pslot=pslot:
                              t_.transpose(
                                  pslot,
                                  h4_sb[:, ch * HID + hc * 128:
                                        ch * HID + (hc + 1) * 128],
                                  identb))
                    p.wait("vector", "T", tv)
                    p.op("vector", lambda v, hc=hc, pslot=pslot:
                         v.tensor_copy(out=hT_s[:, hc * 128:(hc + 1) * 128],
                                       in_=pslot))
                p.wait("tensor", "V")
                for og in range(2):
                    ps = pbC if og == 0 else pbD
                    p.raw("tensor", lambda t_, og=og, ps=ps: t_.matmul(
                        ps, onesr1, rows2[0:1, GCOLS + og * 512:GCOLS + (og + 1) * 512],
                        start=True, stop=False))
                    for hc in range(32):
                        hh, hcl = hc // 16, hc % 16
                        fn = lambda t_, hc=hc, hh=hh, hcl=hcl, og=og, ps=ps: \
                            t_.matmul(
                                ps, hT_s[:, hc * 128:(hc + 1) * 128],
                                w2slab[hh][:, hcl * 1024 + og * 512:
                                           hcl * 1024 + (og + 1) * 512],
                                start=False, stop=(hc == 31))
                        (p.op if hc == 31 else p.raw)("tensor", fn)
                    tvo = p.cnt["T"]
                    p.wait("vector", "T", tvo)
                    p.wait("vector", gq2, gv)
                    p.op("vector", lambda v, og=og, ps=ps, ch=ch:
                         v.tensor_scalar(
                             out=orows[:, og * 512:(og + 1) * 512], in0=ps,
                             scalar1=g_s[ch % 2], scalar2=None,
                             op0=ALU.mult))
                ov = p.cnt["V"]
                gslot_ov[ch % 2] = ov
                p.wait("sync", "V", ov)
                p.dma("sync", orows_out[e][ch * 128:(ch + 1) * 128, :],
                      orows, q="Do")
                p.wait("vector", "Do")
                p.wait("tensor", "V")
                if e == 0 and ch == 0:
                    p.mark("o00")
            if True:
                p.mark(f"e{e}")

        p.wait("gpsimd", "D")
        p.wait("gpsimd", "Dw1")
        p.wait("gpsimd", "Dw2")
        p.wait("gpsimd", "Do")
        p.wait("gpsimd", "Dbr")
        p.wait("gpsimd", "Db0")
        p.wait("gpsimd", "Db1")
        p.wait("gpsimd", "Dga")
        p.wait("gpsimd", "Dgb")

        p.finish(blk)

    nc._prog_cnt = dict(p.cnt)
    return nc


def build_launch_c():
    nc = bass.Bass("TRN2", target_bir_lowering=False)
    x1i = nc.dram_tensor("x1i", [N, C], F32, kind="ExternalInput").ap()
    a_in = nc.dram_tensor("ca", [N, C], F32, kind="ExternalInput").ap()
    b_in = nc.dram_tensor("cb", [N, C], F32, kind="ExternalInput").ap()
    y_out = nc.dram_tensor("y", [N, C], F32, kind="ExternalOutput").ap()

    p = Prog(nc)
    for s in ("D", "V"):
        p.sem(s)
    ctx = ExitStack()
    with ctx:
        en = ctx.enter_context

        def sb(name, shape, dt):
            return en(nc.sbuf_tensor(name, shape, dt)).ap()

        ta = [sb(f"ta{i}", [128, C], F32) for i in range(2)]
        tb = [sb(f"tb{i}", [128, C], F32) for i in range(2)]
        tx = [sb(f"tx{i}", [128, C], F32) for i in range(2)]
        blk = en(nc.Block())
        for t in range(NTILE):
            slot = t % 2
            if t >= 2:
                p.wait("sync", "D")
            d1 = p.dma("sync", ta[slot], a_in[t * 128:(t + 1) * 128, :])
            d1 = p.dma("sync", tb[slot], b_in[t * 128:(t + 1) * 128, :])
            d1 = p.dma("sync", tx[slot], x1i[t * 128:(t + 1) * 128, :])
            p.wait("vector", "D", d1)
            p.op("vector", lambda v, slot=slot: v.tensor_tensor(
                out=ta[slot], in0=ta[slot], in1=tb[slot], op=ALU.add))
            p.wait("vector", "V")
            yv = p.op("vector", lambda v, slot=slot: v.tensor_tensor(
                out=tx[slot], in0=tx[slot], in1=ta[slot], op=ALU.add))
            p.wait("sync", "V", yv)
            p.dma("sync", y_out[t * 128:(t + 1) * 128, :], tx[slot])
        p.wait("gpsimd", "D")
        p.finish(blk)

    nc._prog_cnt = dict(p.cnt)
    return nc


def _host_prep_b0(res_a):
    e1l, e2l = [], []
    for b in range(8):
        meta = np.asarray(res_a[b]["meta"])
        e1l.append(meta[:, 0::4].T.reshape(-1))
        e2l.append(meta[:, 1::4].T.reshape(-1))
    e1g = np.concatenate(e1l); e2g = np.concatenate(e2l)
    easg_flat = np.stack([e1g, e2g], 1).reshape(-1)
    easg = np.ascontiguousarray(
        easg_flat.reshape(JW, 128).T).astype(np.float32)

    tril = np.triu(np.ones((128, 128), np.float32), 1)
    eoff = np.ascontiguousarray(
        np.tile((np.arange(E) * 1024).astype(np.float32), JW)[None, :])
    q = np.arange(128)
    maps = []
    for c in range(8):
        mine = np.isin(q % 16, [2 * c, 2 * c + 1]).astype(np.float32)
        maps.append({
            "easg": easg,
            "tril": tril,
            "trilmy": np.ascontiguousarray(tril * mine[:, None]),
            "onesmy": np.ascontiguousarray(mine.reshape(128, 1)),
            "minemask": np.ascontiguousarray(mine.reshape(128, 1)),
            "eoffrow": eoff,
        })
    return maps


def _host_slots(res_b0):
    slots = []
    for c in range(8):
        idxr = np.asarray(res_b0[c]["idxr"])
        m = np.arange(N)
        sl = np.empty((N, 2), np.int64)
        for k in range(2):
            sl[:, k] = idxr[16 * (m % 8) + 2 * c + k, m >> 3].astype(np.int64)
        slots.append(sl)
    return slots


def _host_prep_b1(res_a, slots, inputs):
    b1 = np.ascontiguousarray(np.asarray(inputs["L0_ffn_b1"]),
                              dtype=np.float32)
    b2 = np.ascontiguousarray(np.asarray(inputs["L0_ffn_b2"]),
                              dtype=np.float32)
    g1l, g2l, x2l = [], [], []
    for b in range(8):
        meta = np.asarray(res_a[b]["meta"])
        g1l.append(meta[:, 2::4].T.reshape(-1))
        g2l.append(meta[:, 3::4].T.reshape(-1))
        x2l.append(np.asarray(res_a[b]["x2n"]))
    g1g = np.concatenate(g1l); g2g = np.concatenate(g2l)
    garr = np.stack([g1g, g2g], 1)
    x2g = np.concatenate(x2l, 0)

    wmap = {}
    for e in range(8):
        wmap[f"w1r{e}"] = np.asarray(res_a[e]["w1b"])
        wmap[f"w2r{e}"] = np.asarray(res_a[e]["w2b"])
    ident = np.eye(128, dtype=np.float32)

    maps = []
    for c in range(8):
        xmine = x2g[c::8]
        gmine = garr[c::8]
        sl = slots[c]
        buf = np.zeros((E * 1024, C), x2g.dtype)
        gt = np.zeros((E * 1024, 1), np.float32)
        for k in range(2):
            kept = sl[:, k] < E * 1024
            buf[sl[kept, k]] = xmine[kept]
            gt[sl[kept, k], 0] = gmine[kept, k]
        maps.append({
            "buf": buf, "gt": gt, "b1r": b1, "b2r": b2,
            "ident": ident, **wmap,
        })
    return maps


def _host_prep_c(res_a, res_b1, slots):
    x1l = [np.asarray(res_a[b]["x1"]) for b in range(8)]
    x1g = np.concatenate(x1l, 0)
    maps = []
    for c in range(8):
        orows = np.concatenate(
            [np.asarray(res_b1[c][f"orows{e}"]) for e in range(8)], 0)
        sl = slots[c]
        contrib = np.zeros((2, N, C), np.float32)
        for k in range(2):
            kept = sl[:, k] < E * 1024
            contrib[k, kept] = orows[sl[kept, k]]
        maps.append({
            "x1i": np.ascontiguousarray(x1g[c::8]),
            "ca": contrib[0], "cb": contrib[1],
        })
    return maps


def kernel(**inputs):
    nc_a = build_launch_a()
    res_a = run_bass_kernel_spmd(nc_a, _host_prep_a(inputs),
                                 core_ids=list(range(8))).results

    nc_b0 = build_launch_b0()
    res_b0 = run_bass_kernel_spmd(nc_b0, _host_prep_b0(res_a),
                                  core_ids=list(range(8))).results
    slots = _host_slots(res_b0)

    nc_b1 = build_launch_b1()
    res_b1 = run_bass_kernel_spmd(nc_b1, _host_prep_b1(res_a, slots, inputs),
                                  core_ids=list(range(8))).results

    nc_c = build_launch_c()
    res_c = run_bass_kernel_spmd(nc_c, _host_prep_c(res_a, res_b1, slots),
                                 core_ids=list(range(8))).results

    y = np.empty((8 * N, C), np.float32)
    for c in range(8):
        y[c::8] = np.asarray(res_c[c]["y"])
    return y.reshape(8, N, C)
